# version 05
# speedup vs baseline: 1.7382x; 1.0078x over previous
import contextlib

import numpy as np
import ml_dtypes
import concourse.bass as bass
import concourse.mybir as mybir
from concourse.bass_utils import run_bass_kernel_spmd

F32 = mybir.dt.float32
BF16 = mybir.dt.bfloat16
I32 = mybir.dt.int32
AF = mybir.ActivationFunctionType
ALU = mybir.AluOpType
AX = mybir.AxisListType

NCORES = 8
S = 8192
D = 2048
TPC = S // NCORES
NH = 16
HD = 128
HPC = NH // NCORES
BF = np.dtype(ml_dtypes.bfloat16)


_ENG2BLOCK = {"pe": "tensor", "dve": "vector", "act": "scalar",
              "pool": "gpsimd", "sp": "sync"}


class _Step:
    __slots__ = ("eng", "fn", "deps", "inc_amt", "marked", "mseq", "slot")

    def __init__(self, eng, fn, deps, inc_amt, slot):
        self.eng = eng
        self.fn = fn
        self.deps = deps
        self.inc_amt = inc_amt
        self.slot = slot
        self.marked = False
        self.mseq = None

    @property
    def skey(self):
        return self.eng if self.slot is None else ("d", self.slot)


class Plan:

    def __init__(self):
        self.steps = []

    def add(self, eng, fn, deps=(), dma=False, slot=None):
        assert (slot is not None) == bool(dma)
        st = _Step(eng, fn, [d for d in deps if d is not None],
                   16 if dma else 1, slot)
        self.steps.append(st)
        return st

    AUTO_CHAIN = ("dve", "act", "pool")

    def finalize(self):
        prev = {}
        for st in self.steps:
            if st.eng in self.AUTO_CHAIN and st.inc_amt == 1:
                if st.eng in prev:
                    st.deps.append(prev[st.eng])
                prev[st.eng] = st
        for st in self.steps:
            if st.inc_amt != 1:
                st.marked = True
            for d in st.deps:
                if d.eng != st.eng or d.inc_amt != 1 or d.eng in self.AUTO_CHAIN:
                    d.marked = True
        cnt = {}
        for st in self.steps:
            if st.marked:
                k = st.skey
                cnt[k] = cnt.get(k, 0) + st.inc_amt
                st.mseq = cnt[k]
        self.final_counts = cnt

    def emit(self, nc, block, es):
        self.finalize()
        skeys = sorted({st.skey for st in self.steps if st.marked}, key=str)
        sems = {k: es.enter_context(nc.semaphore(f"s{j}"))
                for j, k in enumerate(skeys)}

        def run_engine(eng_name):
            def body(engobj):
                waited = {}
                for st in self.steps:
                    if st.eng != eng_name:
                        continue
                    for d in st.deps:
                        if (d.eng == st.eng and d.inc_amt == 1
                                and d.eng not in Plan.AUTO_CHAIN):
                            continue
                        assert d.mseq is not None, "dep not marked"
                        k = d.skey
                        if waited.get(k, 0) < d.mseq:
                            engobj.wait_ge(sems[k], d.mseq)
                            waited[k] = d.mseq
                    if st.fn is not None:
                        instr = st.fn(engobj)
                        if st.marked:
                            instr.then_inc(sems[st.skey], st.inc_amt)
                    else:
                        assert not st.marked
            return body

        for eng_name, blk_name in _ENG2BLOCK.items():
            if any(st.eng == eng_name for st in self.steps):
                getattr(block, blk_name)(run_engine(eng_name))


class Ctx:

    def __init__(self, nc, es):
        self.nc = nc
        self.es = es

    def sbuf(self, name, shape, dtype):
        return self.es.enter_context(self.nc.sbuf_tensor(name, shape, dtype))

    def psum(self, name, shape, dtype):
        return self.es.enter_context(self.nc.psum_tensor(name, shape, dtype))

    def sems(self):
        return {e: self.es.enter_context(self.nc.semaphore("s_" + e))
                for e in _ENG2BLOCK}

    def block(self):
        return self.es.enter_context(self.nc.Block())


def _bf16(a):
    return np.asarray(a).astype(BF)


def _ident_f32():
    return np.eye(128, dtype=np.float32)


def _ident_bf16():
    return np.eye(128, dtype=BF)



def build_p1(stage=4):
    nc = bass.Bass("TRN2", target_bir_lowering=False)
    x_t = nc.dram_tensor("x_t", [TPC, D], F32, kind="ExternalInput").ap()
    w1b = nc.dram_tensor("w1b", [128, D], F32, kind="ExternalInput").ap()
    idf = nc.dram_tensor("idf", [128, 128], F32, kind="ExternalInput").ap()
    idb = nc.dram_tensor("idb", [128, 128], BF16, kind="ExternalInput").ap()
    xfm = nc.dram_tensor("xfm", [16, 128, TPC], F32, kind="ExternalOutput").ap()
    xnfm = nc.dram_tensor("xnfm", [16, 128, TPC], BF16, kind="ExternalOutput").ap()
    x1fm = nc.dram_tensor("x1fm", [16, 128, TPC], BF16, kind="ExternalOutput").ap()
    rstd2c = nc.dram_tensor("rstd2c", [128, 8], F32, kind="ExternalOutput").ap()

    NT = TPC // 128
    p = Plan()
    with contextlib.ExitStack() as es:
        cx = Ctx(nc, es)
        xt = cx.sbuf("xt", [128, 2, D], F32)
        sq_scr = cx.sbuf("sq_scr", [128, D], F32)
        xn_tm = cx.sbuf("xn_tm", [128, 2, D], F32)
        x1_tm = cx.sbuf("x1_tm", [128, 2, D], BF16)
        w1s = cx.sbuf("w1s", [128, D], F32)
        ids_f = cx.sbuf("ids_f", [128, 128], F32)
        ids_b = cx.sbuf("ids_b", [128, 128], BF16)
        stats = cx.sbuf("stats", [128, 8 * 8], F32)
        xfm_sb = cx.sbuf("xfm_sb", [128, 16, TPC], F32)
        xnfm_sb = cx.sbuf("xnfm_sb", [128, 16, TPC], BF16)
        x1fm_sb = cx.sbuf("x1fm_sb", [128, 16, TPC], BF16)
        trx = cx.psum("trx", [128, 2, 512], F32)
        trn = cx.psum("trn", [128, 2, 512], F32)
        trb = cx.psum("trb", [128, 2, 1024], BF16)
        block = cx.block()

        consts = cx.sbuf("consts", [128, 8], F32)
        cmap = {}

        def cns(val):
            if val not in cmap:
                j = len(cmap)
                ap = consts[:, j:j + 1]
                st = p.add("pool", lambda e, o=ap, v=val: e.memset(o, v))
                cmap[val] = (ap, st)
            return cmap[val]

        def col(j, i):
            return stats[:, j * 8 + i:j * 8 + i + 1]

        Lw = p.add("sp", lambda e: e.dma_start(out=w1s[:, :], in_=w1b), dma=True, slot="w1")
        Lif = p.add("sp", lambda e: e.dma_start(out=ids_f[:, :], in_=idf), dma=True, slot="idf")
        Lib = p.add("sp", lambda e: e.dma_start(out=ids_b[:, :], in_=idb), dma=True, slot="idb")

        xt_readers = [[], []]
        xn_readers = [[], []]
        x1_readers = [[], []]
        tr_evac = {"x": [None, None], "n": [None, None], "b": [None, None]}
        last_evac = {}
        for i in range(NT):
            par = i % 2
            xti = xt[:, par, :]
            Ld = p.add("sp", lambda e, o=xti, s=i: e.dma_start(
                out=o, in_=x_t[s * 128:(s + 1) * 128, :]),
                deps=xt_readers[par], dma=True, slot=("xt", par))
            xt_readers[par] = []
            A1 = p.add("act", lambda e, o=sq_scr[:, :], src=xti, a=col(1, i):
                       e.activation(o, src, AF.Square, accum_out=a), deps=[Ld])
            V1 = p.add("dve", lambda e, o=col(0, i), src=xti:
                       e.reduce_sum(o, src, axis=AX.X), deps=[Ld])
            p.add("dve", lambda e, o=col(2, i), src=col(0, i):
                  e.tensor_single_scalar(o, src, 1.0 / D, ALU.mult), deps=[V1])
            p.add("dve", lambda e, o=col(3, i), a=col(2, i):
                  e.tensor_tensor(o, a, a, ALU.mult))
            p.add("dve", lambda e, o=col(4, i), src=col(1, i):
                  e.tensor_single_scalar(o, src, 1.0 / D, ALU.mult), deps=[A1])
            V4 = p.add("dve", lambda e, o=col(4, i), b=col(3, i):
                       e.tensor_sub(o, o, b))
            eps1, eps1_st = cns(1e-5)
            mh, mh_st = cns(-0.5)
            A2 = p.add("act", lambda e, o=col(3, i), src=col(4, i), b=eps1:
                       e.activation(o, src, AF.Ln, bias=b),
                       deps=[V4, eps1_st])
            A3 = p.add("act", lambda e, o=col(6, i), src=col(3, i), sc=mh:
                       e.activation(o, src, AF.Exp, scale=sc), deps=[mh_st])
            V5 = p.add("dve", lambda e, o=col(5, i), m=col(2, i), r=col(6, i):
                       e.tensor_scalar(o, m, r, -1.0, ALU.mult, ALU.mult),
                       deps=[A3])
            V6 = p.add("dve", lambda e, o=xn_tm[:, par, :], src=xti,
                       r=col(6, i), mr=col(5, i):
                       e.tensor_scalar(o, src, r, mr, ALU.mult, ALU.add),
                       deps=[A3] + xn_readers[par])
            xn_readers[par] = []
            V7a = p.add("dve", lambda e, o=col(3, i), src=col(4, i):
                        e.tensor_single_scalar(o, src, 1e-5, ALU.add),
                        deps=[A3])
            p.add("dve", lambda e, o=col(3, i): e.reciprocal(o, o))
            V7c = p.add("dve", lambda e, o=col(3, i), v=col(4, i):
                        e.tensor_mul(o, o, v))
            eps2, eps2_st = cns(1e-6)
            A4 = p.add("act", lambda e, o=col(3, i), b=eps2:
                       e.activation(o, o, AF.Ln, bias=b),
                       deps=[V7c, eps2_st])
            A5 = p.add("act", lambda e, o=col(7, i), src=col(3, i), sc=mh:
                       e.activation(o, src, AF.Exp, scale=sc))
            if stage < 2:
                xt_readers[par] = [A1, V1]
                last_A5 = A5
                continue
            V8 = p.add("dve", lambda e, o=x1_tm[:, par, :], src=xn_tm[:, par, :],
                       r=col(7, i): e.tensor_single_scalar(o, src, r, ALU.mult),
                       deps=[A5] + x1_readers[par])
            x1_readers[par] = []
            V9 = p.add("dve", lambda e, o=x1_tm[:, par, :]:
                       e.tensor_mul(o, o, w1s[:, :]), deps=[Lw])
            xt_readers[par] = [A1, V6]
            for c in (range(16) if stage >= 3 else []):
                cp = c % 2
                cs = slice(c * 128, (c + 1) * 128)
                ts_ = slice(i * 128, (i + 1) * 128)
                Tx = p.add("pe", lambda e, o=trx[:, cp, 0:128], src=xti, csl=cs:
                           e.transpose(o, src[:, csl], ids_f[:, :]),
                           deps=[Ld, Lif, tr_evac["x"][cp]])
                Ex = p.add("dve", lambda e, o=xfm_sb[:, c, ts_], src=trx[:, cp, 0:128]:
                           e.tensor_copy(o, src), deps=[Tx])
                tr_evac["x"][cp] = Ex
                Tn = p.add("pe", lambda e, o=trn[:, cp, 0:128], src=xn_tm[:, par, :],
                           csl=cs: e.transpose(o, src[:, csl], ids_f[:, :]),
                           deps=[V6, tr_evac["n"][cp]])
                En = p.add("act", lambda e, o=xnfm_sb[:, c, ts_], src=trn[:, cp, 0:128]:
                           e.copy(o, src), deps=[Tn])
                tr_evac["n"][cp] = En
                Tb = p.add("pe", lambda e, o=trb[:, cp, 0:128], src=x1_tm[:, par, :],
                           csl=cs: e.transpose(o, src[:, csl], ids_b[:, :]),
                           deps=[V9, Lib, tr_evac["b"][cp]])
                Eb = p.add("dve", lambda e, o=x1fm_sb[:, c, ts_], src=trb[:, cp, 0:128]:
                           e.tensor_copy(o, src), deps=[Tb])
                tr_evac["b"][cp] = Eb
                last_evac[("x", c)] = Ex
                last_evac[("n", c)] = En
                last_evac[("b", c)] = Eb
            if stage >= 3:
                xt_readers[par].append(Tx)
                xn_readers[par].append(Tn)
                x1_readers[par].append(Tb)
            last_A5 = A5

        stores = []
        for c in (range(16) if stage >= 4 else []):
            stores.append(p.add("sp", lambda e, cc=c: e.dma_start(
                out=xfm[cc], in_=xfm_sb[:, cc, :]),
                deps=[last_evac[("x", c)]], dma=True, slot="st"))
            stores.append(p.add("sp", lambda e, cc=c: e.dma_start(
                out=xnfm[cc], in_=xnfm_sb[:, cc, :]),
                deps=[last_evac[("n", c)]], dma=True, slot="st"))
            stores.append(p.add("sp", lambda e, cc=c: e.dma_start(
                out=x1fm[cc], in_=x1fm_sb[:, cc, :]),
                deps=[last_evac[("b", c)]], dma=True, slot="st"))
        stores.append(p.add("sp", lambda e: e.dma_start(
            out=rstd2c, in_=stats[:, 7 * 8:8 * 8]), deps=[last_A5],
            dma=True, slot="st"))
        p.add("sp", None, deps=[stores[-1]])
        p.emit(nc, block, cx.es)
    return nc



SCL = float(1.0 / np.sqrt(128.0))
NEG = -30000.0


def _attn_causal(p, shared, xnb, wq_ap, wk_ap, wv_ap, out_ap, rope,
                 cos_ap, sin_ap, nb=16):
    sg = shared["sg"]
    ob = shared["ob"]
    lrow = shared["lrow"]
    vtr = shared["vtr"]
    bank_last = shared["bank_last"]
    buf_last = shared["buf_last"]
    cns = shared["cns"]
    xin = shared["xin"]
    w_sb = shared["w_sb"]
    q_sb = shared["q_sb"]
    k_sb = shared["k_sb"]
    v_tm = shared["v_tm"]
    cos_sb = shared["cos_sb"]
    sin_sb = shared["sin_sb"]
    rsc = shared["rsc"]
    masksb = shared["masksb"]
    maskf = shared["maskf"]
    pt_sb = shared["pt_sb"]
    oev = shared["oev"]
    vstage = shared["vstage"]
    dscr = shared["dscr"]
    mx = shared["mx"]
    fl = shared["fl"]
    flbf = shared["flbf"]
    msrow = shared["msrow"]
    rlrow = shared["rlrow"]
    rlf = shared["rlf"]
    rlb = shared["rlb"]

    def bdep(k):
        return list(bank_last.get(k, ()))

    wsteps = []
    for t_i, w_ap in enumerate((wq_ap, wk_ap, wv_ap)):
        for kc in range(16):
            dep = [shared.get("last_pe")] if not wsteps else []
            wsteps.append(p.add("pool", lambda e, o=w_sb[:, t_i, kc, :],
                                src=w_ap[kc]: e.dma_start(out=o, in_=src),
                                deps=dep, dma=True, slot="w"))
    w_ready = wsteps[-1]

    for h in range(2):
        hc = slice(h * 128, (h + 1) * 128)
        proj_last = {}

        def padd(eng, fn, deps=()):
            st = p.add(eng, fn, deps)
            proj_last[eng] = st
            return st

        pmap = {0: sg[0][:, 0:512], 1: sg[0][:, 512:1024],
                2: sg[1][:, 0:512]}
        pkey = {0: ("sg0", 0), 1: ("sg0", 1), 2: ("sg1", 0)}
        permmap = {0: sg[1][:, 512:1024], 1: ob[0][:, :]}
        permkey = {0: ("sg1", 1), 1: ("ob", 0)}
        for n in range(nb):
            par = n % 2
            ncol = slice(n * 512, (n + 1) * 512)
            Lx = p.add("sp", lambda e, o=xin[:, par, :, :], src=xnb[n]:
                       e.dma_start(out=o, in_=src),
                       deps=buf_last.pop(("xin", par), ()),
                       dma=True, slot=("xin", par))
            if rope:
                Lc = p.add("sp", lambda e, o=cos_sb[:, par, :],
                           src=cos_ap[:, ncol]: e.dma_start(out=o, in_=src),
                           deps=buf_last.pop(("cos", par), ()),
                           dma=True, slot=("cos", par))
                Ls = p.add("sp", lambda e, o=sin_sb[:, par, :],
                           src=sin_ap[:, ncol]: e.dma_start(out=o, in_=src),
                           deps=buf_last.pop(("sin", par), ()),
                           dma=True, slot=("sin", par))
            mm = {}
            for m in range(3):
                for kc in range(16):
                    mm[m] = p.add("pe", lambda e, o=pmap[m],
                                  w=w_sb[:, m, kc, hc],
                                  x=xin[:, par, kc, :], st=(kc == 0),
                                  sp_=(kc == 15):
                                  e.matmul(o, w, x, start=st, stop=sp_),
                                  deps=[Lx, w_ready] + bdep(pkey[m])
                                  if kc == 0 else [])
            xin_readers = [mm[2]]
            cos_readers, sin_readers = [], []
            for m, dest in ((0, q_sb), (1, k_sb)):
                ps = pmap[m]
                if rope:
                    qraw = shared["qraw"][:, m, :]
                    qsw = shared["qsw"][:, m, :]
                    c0 = padd("act", lambda e, o=qraw, src=ps:
                              e.copy(o, src),
                              deps=[mm[m]] +
                              list(buf_last.pop(("qraw", m), ())))
                    bank_last[pkey[m]] = [c0]
                    pmm = p.add("pe", lambda e, o=permmap[m],
                                w=shared["permm_sb"], x=qraw:
                                e.matmul(o, w[:, :], x, start=True,
                                         stop=True),
                                deps=[c0, shared["permm_ready"]] +
                                bdep(permkey[m]))
                    c1 = padd("act", lambda e, o=qsw, src=permmap[m]:
                              e.copy(o, src),
                              deps=[pmm] +
                              list(buf_last.pop(("qsw", m), ())))
                    bank_last[permkey[m]] = [c1]
                    c_, s_ = cos_sb[:, par, :], sin_sb[:, par, :]
                    ra = rsc[:, par, m, 0, :]
                    rb = rsc[:, par, m, 1, :]
                    e1 = padd("dve", lambda e, o=ra, a=qraw, b=c_:
                              e.tensor_mul(o, a, b),
                              deps=[c0, Lc] +
                              list(buf_last.pop(("ra", par, m), ())))
                    e2 = padd("pool", lambda e, o=rb, a=qsw, b=s_:
                              e.tensor_mul(o, a, b),
                              deps=[c1, Ls] +
                              list(buf_last.pop(("rb", par, m), ())))
                    e3 = padd("dve", lambda e, o=dest[:, ncol], a=ra, b=rb:
                              e.tensor_add(o, a, b), deps=[e2])
                    buf_last[("ra", par, m)] = [e3]
                    buf_last[("rb", par, m)] = [e3]
                    buf_last[("qraw", m)] = [pmm, e1]
                    buf_last[("qsw", m)] = [e2]
                    cos_readers.append(e1)
                    sin_readers.append(e2)
                else:
                    ev = padd("act", lambda e, o=dest[:, ncol], src=ps:
                              e.copy(o, src), deps=[mm[m]])
                    bank_last[pkey[m]] = [ev]
            if rope:
                buf_last[("cos", par)] = cos_readers
                buf_last[("sin", par)] = sin_readers
            vs = padd("act", lambda e, o=vstage[:, :], src=pmap[2]:
                      e.copy(o, src),
                      deps=[mm[2]] + list(buf_last.pop("vstage", ())))
            bank_last[pkey[2]] = [vs]
            for u in range(4):
                tv = p.add("pe", lambda e, o=vtr[:, u * 128:(u + 1) * 128],
                           src=vstage[:, u * 128:(u + 1) * 128],
                           idn=shared["ids_b"]:
                           e.transpose(o, src, idn[:, :]),
                           deps=[vs, shared["idb_ready"]] +
                           bdep("vtr") if u == 0 else [vs])
            ev = padd("dve", lambda e, o=v_tm[:, n * 4:(n + 1) * 4, :]
                      .rearrange("p a b -> p (a b)"), src=vtr[:, 0:512]:
                      e.tensor_copy(o, src), deps=[tv])
            bank_last["vtr"] = [ev]
            buf_last["vstage"] = [tv]
            xin_readers.append(tv)
            buf_last[("xin", par)] = xin_readers
        qk_ready = [s for s in proj_last.values()]

        sscale, sscale_st = cns(SCL)
        for T in range(nb):
            nch = T + 1
            q512 = q_sb[:, T * 512:(T + 1) * 512]
            obk = T % 2
            first = True
            lfirst = True
            pend = []
            pcols = [(kc, u) for kc in range(nch) for u in range(4)]
            for ci in range(0, len(pcols), 2):
                gi = (ci // 2) % 2
                slot = (ci // 2) % 3
                cols = []
                MM = None
                for half in range(2):
                    kc, u = pcols[ci + half]
                    col = kc * 4 + u
                    cols.append((col, kc == T, u))
                    MM = p.add("pe", lambda e,
                               o=sg[gi][:, half * 512:(half + 1) * 512],
                               kt=k_sb[:, col * 128:(col + 1) * 128],
                               x=q512:
                               e.matmul(o, kt, x, start=True, stop=True),
                               deps=(bdep(("sgp", gi)) +
                                     (qk_ready if ci == 0 else []))
                               if half == 0 else [])
                ES = p.add("act", lambda e, o=pt_sb[:, slot, :],
                           src=sg[gi][:, :], sc=sscale:
                           e.activation(o, src, AF.Exp, scale=sc),
                           deps=[MM, sscale_st] +
                           list(buf_last.pop(("pt_sb", slot), ())))
                bank_last[("sgp", gi)] = [ES]
                lastP = ES
                for half in range(2):
                    col, isdiag, u = cols[half]
                    if isdiag:
                        lastP = p.add("dve", lambda e,
                                      o=pt_sb[:, slot,
                                              half * 512:(half + 1) * 512],
                                      b=masksb[:, u, :]:
                                      e.tensor_mul(o, o, b),
                                      deps=[ES, shared["masksb_ready"]])
                pend.append((lastP, slot, cols))
                if len(pend) > 2:
                    pP, pslot, pcolsv = pend.pop(0)
                    for half, (col, _, _) in enumerate(pcolsv):
                        LM = p.add("pe", lambda e, o=lrow[0:1, :],
                                   w=shared["onescl_sb"],
                                   x=pt_sb[:, pslot,
                                           half * 512:(half + 1) * 512],
                                   st=lfirst:
                                   e.matmul(o, w[:, 0:1], x, start=st,
                                            stop=False),
                                   deps=[pP] + (bdep("lrow") +
                                                [shared["onescl_ready"]]
                                                if lfirst else []))
                        lfirst = False
                        AV = p.add("pe", lambda e, o=ob[obk][:, :],
                                   v=v_tm[:, col, :],
                                   x=pt_sb[:, pslot,
                                           half * 512:(half + 1) * 512],
                                   st=first:
                                   e.matmul(o, v, x, start=st, stop=False),
                                   deps=bdep(("ob", obk)) if first else [])
                        first = False
                    buf_last[("pt_sb", pslot)] = [AV]
            while pend:
                pP, pslot, pcolsv = pend.pop(0)
                tail = not pend
                for half, (col, _, _) in enumerate(pcolsv):
                    last = tail and half == 1
                    LM = p.add("pe", lambda e, o=lrow[0:1, :],
                               w=shared["onescl_sb"],
                               x=pt_sb[:, pslot,
                                       half * 512:(half + 1) * 512],
                               st=lfirst, sp_=last:
                               e.matmul(o, w[:, 0:1], x, start=st,
                                        stop=sp_),
                               deps=[pP] + (bdep("lrow") +
                                            [shared["onescl_ready"]]
                                            if lfirst else []))
                    lfirst = False
                    AV = p.add("pe", lambda e, o=ob[obk][:, :],
                               v=v_tm[:, col, :],
                               x=pt_sb[:, pslot,
                                       half * 512:(half + 1) * 512],
                               st=first, sp_=last:
                               e.matmul(o, v, x, start=st, stop=sp_),
                               deps=bdep(("ob", obk)) if first else [])
                    first = False
                buf_last[("pt_sb", pslot)] = [AV]
            LE = p.add("dve", lambda e, o=rlf[0:1, :], src=lrow[0:1, :]:
                       e.tensor_copy(o, src),
                       deps=[LM] + list(buf_last.pop("rlf", ())))
            bank_last["lrow"] = [LE]
            LR = p.add("dve", lambda e, o=rlf[0:1, :]:
                       e.reciprocal(o, o))
            LC = p.add("act", lambda e, o=rlrow[0:1, :], src=rlf[0:1, :]:
                       e.copy(o, src),
                       deps=[LR] + list(buf_last.pop("rlrow", ())))
            buf_last["rlf"] = [LC]
            rbk = 1 - obk
            RBM = p.add("pe", lambda e, o=ob[rbk][:, :],
                        w=shared["onesr_sb"], x=rlrow[0:1, :]:
                        e.matmul(o, w[0:1, :], x, start=True, stop=True),
                        deps=[LC, shared["onesr_ready"]] + bdep(("ob", rbk)))
            buf_last["rlrow"] = [RBM]
            RBE = p.add("dve", lambda e, o=rlb[:, :], src=ob[rbk][:, :]:
                        e.tensor_copy(o, src),
                        deps=[RBM] + list(buf_last.pop("rlb", ())))
            bank_last[("ob", rbk)] = [RBE]
            OE = p.add("dve", lambda e, o=oev[:, obk, :],
                       a=ob[obk][:, :], b=rlb[:, :]: e.tensor_mul(o, a, b),
                       deps=[AV, RBE] + list(buf_last.pop(("oev", obk), ())))
            bank_last[("ob", obk)] = [OE]
            buf_last["rlb"] = [OE]
            OS = p.add("pool", lambda e,
                       o=out_ap[h][:, T * 512:(T + 1) * 512],
                       src=oev[:, obk, :]: e.dma_start(out=o, in_=src),
                       deps=[OE], dma=True, slot=("st", obk))
            buf_last[("oev", obk)] = [OS]
            shared["stores"][obk] = OS
            shared["last_pe"] = AV


def build_p2(seq=S, oscale=True):
    nb = seq // 512
    nc = bass.Bass("TRN2", target_bir_lowering=False)
    xnb = nc.dram_tensor("xnb", [nb, 128, 16, 512], BF16,
                         kind="ExternalInput").ap()
    x1b = nc.dram_tensor("x1b", [nb, 128, 16, 512], BF16,
                         kind="ExternalInput").ap()
    wq0 = nc.dram_tensor("wq0", [16, 128, 256], F32, kind="ExternalInput").ap()
    wk0 = nc.dram_tensor("wk0", [16, 128, 256], F32, kind="ExternalInput").ap()
    wv0 = nc.dram_tensor("wv0", [16, 128, 256], F32, kind="ExternalInput").ap()
    wqf = nc.dram_tensor("wqf", [16, 128, 256], F32, kind="ExternalInput").ap()
    wkf = nc.dram_tensor("wkf", [16, 128, 256], F32, kind="ExternalInput").ap()
    wvf = nc.dram_tensor("wvf", [16, 128, 256], F32, kind="ExternalInput").ap()
    cosr = nc.dram_tensor("cosr", [128, seq], BF16, kind="ExternalInput").ap()
    sinr = nc.dram_tensor("sinr", [128, seq], BF16, kind="ExternalInput").ap()
    maskr = nc.dram_tensor("maskr", [128, 4, 512], F32,
                           kind="ExternalInput").ap()
    idb = nc.dram_tensor("idb", [128, 128], BF16, kind="ExternalInput").ap()
    permm = nc.dram_tensor("permm", [128, 128], BF16,
                           kind="ExternalInput").ap()
    onesr = nc.dram_tensor("onesr", [1, 128], BF16,
                           kind="ExternalInput").ap()
    onescl = nc.dram_tensor("onescl", [128, 1], BF16,
                            kind="ExternalInput").ap()
    masksbr = nc.dram_tensor("masksbr", [128, 4, 512], BF16,
                             kind="ExternalInput").ap()
    o0 = nc.dram_tensor("o0", [2, 128, seq], BF16, kind="ExternalOutput").ap()
    of = nc.dram_tensor("of", [2, 128, seq], BF16, kind="ExternalOutput").ap()

    p = Plan()
    with contextlib.ExitStack() as es:
        cx = Ctx(nc, es)
        shared = {}
        shared["sg"] = [cx.psum(f"sg{i}", [128, 1024], F32) for i in range(2)]
        shared["ob"] = [cx.psum(f"ob{i}", [128, 512], F32) for i in range(2)]
        shared["lrow"] = cx.psum("lrow", [1, 512], F32)
        shared["vtr"] = cx.psum("vtr", [128, 1024], BF16)
        shared["bank_last"] = {}
        shared["buf_last"] = {}
        shared["xin"] = cx.sbuf("xin", [128, 2, 16, 512], BF16)
        shared["w_sb"] = cx.sbuf("w_sb", [128, 3, 16, 256], BF16)
        shared["q_sb"] = cx.sbuf("q_sb", [128, seq], BF16)
        shared["k_sb"] = cx.sbuf("k_sb", [128, seq], BF16)
        shared["v_tm"] = cx.sbuf("v_tm", [128, seq // 128, 128], BF16)

        shared["cos_sb"] = cx.sbuf("cos_sb", [128, 2, 512], BF16)
        shared["sin_sb"] = cx.sbuf("sin_sb", [128, 2, 512], BF16)
        shared["rsc"] = cx.sbuf("rsc", [128, 2, 2, 2, 512], BF16)
        shared["maskf"] = cx.sbuf("maskf", [128, 4, 512], F32)
        shared["masksb"] = cx.sbuf("masksb", [128, 4, 512], BF16)
        shared["dscr"] = cx.sbuf("dscr", [128, 512], F32)
        shared["pt_sb"] = cx.sbuf("pt_sb", [128, 3, 1024], BF16)
        shared["oev"] = cx.sbuf("oev", [128, 2, 512], BF16)
        shared["vstage"] = cx.sbuf("vstage", [128, 512], BF16)
        shared["qraw"] = cx.sbuf("qraw", [128, 2, 512], BF16)
        shared["rlrow"] = cx.sbuf("rlrow", [1, 512], BF16)
        shared["rlb"] = cx.sbuf("rlb", [128, 512], F32)
        shared["onesr_sb"] = cx.sbuf("onesr_sb", [1, 128], BF16)
        shared["onescl_sb"] = cx.sbuf("onescl_sb", [128, 1], BF16)
        shared["qsw"] = cx.sbuf("qsw", [128, 2, 512], BF16)
        shared["permm_sb"] = cx.sbuf("permm_sb", [128, 128], BF16)
        shared["mx"] = cx.sbuf("mx", [128, 4, nb], F32)
        shared["fl"] = cx.sbuf("fl", [128, 12], F32)
        shared["flbf"] = cx.sbuf("flbf", [128, 4], BF16)
        shared["msrow"] = cx.sbuf("msrow", [1, 512], BF16)
        shared["rlf"] = cx.sbuf("rlf", [1, 512], F32)
        shared["ids_b"] = cx.sbuf("ids_bs", [128, 128], BF16)
        consts = cx.sbuf("consts", [128, 4], F32)
        shared["stores"] = {0: None, 1: None}
        shared["oscale"] = oscale
        block = cx.block()

        cmap = {}

        def cns(val):
            if val not in cmap:
                j = len(cmap)
                ap = consts[:, j:j + 1]
                st = p.add("pool", lambda e, o=ap, v=val: e.memset(o, v))
                cmap[val] = (ap, st)
            return cmap[val]
        shared["cns"] = cns

        Lm = p.add("sp", lambda e: e.dma_start(out=shared["maskf"][:, :, :],
                                               in_=maskr), dma=True, slot="m")
        Lmb = p.add("sp", lambda e: e.dma_start(
            out=shared["masksb"][:, :, :], in_=masksbr), dma=True, slot="mb")
        Li = p.add("sp", lambda e: e.dma_start(out=shared["ids_b"][:, :],
                                               in_=idb), dma=True, slot="i")
        Lp = p.add("sp", lambda e: e.dma_start(out=shared["permm_sb"][:, :],
                                               in_=permm), dma=True, slot="pm")
        Lr = p.add("sp", lambda e: e.dma_start(out=shared["onesr_sb"][:, :],
                                               in_=onesr), dma=True,
                   slot="onr")
        Lcl = p.add("sp", lambda e: e.dma_start(
            out=shared["onescl_sb"][:, :], in_=onescl), dma=True, slot="onc")
        shared["maskf_ready"] = Lm
        shared["masksb_ready"] = Lmb
        shared["idb_ready"] = Li
        shared["permm_ready"] = Lp
        shared["onesr_ready"] = Lr
        shared["onescl_ready"] = Lcl

        for (xb, wq, wk, wv, out, rope) in (
                (xnb, wq0, wk0, wv0, o0, True),
                (x1b, wqf, wkf, wvf, of, False)):
            _attn_causal(p, shared, xb, wq, wk, wv, out, rope, cosr, sinr,
                         nb=nb)

        p.add("pool", None,
              deps=[shared["stores"][0], shared["stores"][1]])
        p.emit(nc, block, cx.es)
    return nc



def build_p3():
    nc = bass.Bass("TRN2", target_bir_lowering=False)
    ap = lambda n, s, d, k="ExternalInput": nc.dram_tensor(n, s, d, kind=k).ap()
    xfm = ap("xfm", [16, 128, TPC], F32)
    xnfm = ap("xnfm", [16, 128, TPC], BF16)
    o0sl = ap("o0sl", [128, 16, TPC], BF16)
    ofsl = ap("ofsl", [128, 16, TPC], BF16)
    rstd2b = ap("rstd2b", [128, TPC], F32)
    w2pc = ap("w2pc", [128, 16], F32)
    a_b = ap("a_b", [128, 16, 16], F32)
    b_b = ap("b_b", [16, 8192], F32)
    wo0_b = ap("wo0_b", [16, 128, 16, 128], F32)
    wof_b = ap("wof_b", [16, 128, 16, 128], F32)
    wiu_b = ap("wiu_b", [64, 128, 2, 16, 128], F32)
    wd_b = ap("wd_b", [16, 128, 64, 128], F32)
    onescol = ap("onescol", [128, 1], BF16)
    onesrow = ap("onesrow", [1, 128], BF16)
    yfm = ap("yfm", [16, 128, TPC], F32, "ExternalOutput")
    xn2fm = ap("xn2fm", [16, 128, TPC], BF16, "ExternalOutput")

    p = Plan()
    with contextlib.ExitStack() as es:
        cx = Ctx(nc, es)
        pb = [cx.psum(f"pb{i}", [128, 512], F32) for i in range(4)]
        b45 = cx.psum("b45", [16, 1024], F32)
        st2 = cx.psum("st2", [1, 1024], F32)
        gu = cx.sbuf("gu", [128, 64, 512], BF16)
        h_sb = cx.sbuf("h_sb", [128, 16, 512], BF16)
        of_sb = cx.sbuf("of_sb", [128, 16, TPC], BF16)
        o0t = cx.sbuf("o0t", [128, 16, 512], BF16)
        afc = cx.sbuf("afc", [128, 2, 512], BF16)
        wiu_t = cx.sbuf("wiu_t", [128, 2, 2, 16, 128], BF16)
        wd_t = cx.sbuf("wd_t", [128, 2, 16, 128], BF16)
        wo_t = cx.sbuf("wo_t", [128, 2, 16, 128], BF16)
        bt = cx.sbuf("bt", [16, 2, 128], BF16)
        a_sb = cx.sbuf("a_sb", [128, 16, 16], BF16)
        w2s = cx.sbuf("w2s", [128, 16], F32)
        r2b = cx.sbuf("r2b", [128, TPC], BF16)
        xs = cx.sbuf("xs", [128, 2, 512], F32)
        xns = cx.sbuf("xns", [128, 2, 512], BF16)
        g_sb = cx.sbuf("g_sb", [128, 512], BF16)
        ysc = cx.sbuf("ysc", [128, 2, 512], F32)
        ybf = cx.sbuf("ybf", [128, 512], BF16)
        rows = cx.sbuf("rows", [1, 2, 1024], F32)
        rbf = cx.sbuf("rbf", [1, 1024], BF16)
        mods = cx.sbuf("mods", [16, 1024], F32)
        ha_bf = cx.sbuf("ha_bf", [16, 1024], BF16)
        mha = cx.sbuf("mha", [16, TPC], BF16)
        ocol = cx.sbuf("ocol", [128, 1], BF16)
        orow = cx.sbuf("orow", [1, 128], BF16)
        bcst = cx.sbuf("bcst", [128, 2, TPC], BF16)
        consts = cx.sbuf("consts", [128, 4], F32)
        block = cx.block()

        bank_last = {}
        buf_last = {}

        def bdep(k):
            return list(bank_last.get(k, ()))

        cmap = {}

        def cns(val):
            if val not in cmap:
                j = len(cmap)
                capn = consts[:, j:j + 1]
                st = p.add("pool", lambda e, o=capn, v=val: e.memset(o, v))
                cmap[val] = (capn, st)
            return cmap[val]

        Lw2 = p.add("sp", lambda e: e.dma_start(out=w2s[:, :], in_=w2pc),
                    dma=True, slot="w2")
        Lr2 = p.add("pool", lambda e: e.dma_start(out=r2b[:, :], in_=rstd2b),
                    dma=True, slot="r2")
        La = p.add("pool", lambda e: e.dma_start(out=a_sb[:, :, :], in_=a_b),
                   dma=True, slot="a")
        Loc = p.add("sp", lambda e: e.dma_start(out=ocol[:, :], in_=onescol),
                    dma=True, slot="oc")
        Lor = p.add("sp", lambda e: e.dma_start(out=orow[:, :], in_=onesrow),
                    dma=True, slot="or")
        Lof = p.add("sp", lambda e: e.dma_start(out=of_sb[:, :, :], in_=ofsl),
                    dma=True, slot="of")


        AMM = None
        for m in range(16):
            par = m % 2
            Lwf = p.add("pool", lambda e, o=wo_t[:, par, :, :],
                        src=wof_b[m]: e.dma_start(out=o, in_=src),
                        deps=buf_last.pop(("wo", par), ()),
                        dma=True, slot=("wo", par))
            for half in range(2):
                hs = slice(half * 512, (half + 1) * 512)
                hb = half
                MMf = None
                for kc in range(16):
                    MMf = p.add("pe", lambda e, o=pb[hb][:, :],
                                w=wo_t[:, par, kc, :], x=of_sb[:, kc, hs],
                                st=(kc == 0), sp_=(kc == 15):
                                e.matmul(o, w, x, start=st, stop=sp_),
                                deps=([Lof, Lwf] + bdep(("pb", hb)))
                                if kc == 0 else [])
                EVf = p.add("act", lambda e, o=afc[:, par, :],
                            src=pb[hb][:, :]: e.copy(o, src),
                            deps=[MMf] +
                            list(buf_last.pop(("afc", par), ())))
                bank_last[("pb", hb)] = [EVf]
                AMM = p.add("pe", lambda e, o=b45[:, hs],
                            w=a_sb[:, m, :], x=afc[:, par, :],
                            st=(m == 0), sp_=(m == 15):
                            e.matmul(o, w, x, start=st, stop=sp_),
                            deps=[EVf, La] + (bdep("b45")
                                              if m == 0 and half == 0
                                              else []))
                buf_last[("afc", par)] = [AMM]
            buf_last[("wo", par)] = [MMf]
        EVa = p.add("dve", lambda e, o=mods[:, :], src=b45[:, :]:
                    e.tensor_copy(o, src), deps=[AMM])
        bank_last["b45"] = [EVa]
        T1 = p.add("act", lambda e, o=mods[:, :]:
                   e.activation(o, o, AF.Tanh), deps=[EVa])
        T2 = p.add("dve", lambda e, o=mods[:, :]:
                   e.tensor_single_scalar(o, o, 0.1, ALU.mult), deps=[T1])

        stats_done = []
        sy_last = {}
        for ti in range(2):
            ts_ = slice(ti * 512, (ti + 1) * 512)
            H2 = None
            for m in range(16):
                par = m % 2
                Lxn = p.add("sp", lambda e, o=xns[:, par, :],
                            src=xnfm[m][:, ts_]: e.dma_start(out=o, in_=src),
                            deps=buf_last.pop(("xns", par), ()),
                            dma=True, slot=("xn", par))
                H1 = p.add("dve", lambda e, o=h_sb[:, m, :],
                           src=xns[:, par, :], w=w2s[:, m:m + 1]:
                           e.tensor_single_scalar(o, src, w, ALU.mult),
                           deps=[Lxn, Lw2] +
                           list(buf_last.pop(("h", m), ())))
                H2 = p.add("dve", lambda e, o=h_sb[:, m, :],
                           r=r2b[:, ts_]: e.tensor_mul(o, o, r),
                           deps=[Lr2])
                buf_last[("xns", par)] = [H1]
            h_done = H2
            HMM = None
            for m in range(16):
                HMM = p.add("pe", lambda e, o=b45[:, ts_],
                            w=a_sb[:, m, :], x=h_sb[:, m, :],
                            st=(m == 0), sp_=(m == 15):
                            e.matmul(o, w, x, start=st, stop=sp_),
                            deps=([h_done, La] + bdep("b45"))
                            if m == 0 else [])
            EVh = p.add("act", lambda e, o=ha_bf[:, ts_], src=b45[:, ts_]:
                        e.copy(o, src), deps=[HMM])
            bank_last["b45"] = [EVh]
            T3 = p.add("dve", lambda e, o=mha[:, ts_], a=mods[:, ts_],
                       b=ha_bf[:, ts_]: e.tensor_mul(o, a, b),
                       deps=[EVh, T2])
            guf = {}
            for f in range(64):
                par = f % 2
                Lw = p.add("pool", lambda e, o=wiu_t[:, par, :, :, :],
                           src=wiu_b[f]: e.dma_start(out=o, in_=src),
                           deps=buf_last.pop(("wiu", par), ()),
                           dma=True, slot=("wiu", par))
                Lb = p.add("pool", lambda e, o=bt[:, par, :],
                           src=b_b[:, f * 128:(f + 1) * 128]:
                           e.dma_start(out=o, in_=src),
                           deps=buf_last.pop(("bt", par), ()),
                           dma=True, slot=("bt", par))
                gp, up_ = (0, 1) if f % 2 == 0 else (2, 3)
                MG = None
                for kc in range(16):
                    MG = p.add("pe", lambda e, o=pb[gp][:, :],
                               w=wiu_t[:, par, 0, kc, :],
                               x=h_sb[:, kc, :], st=(kc == 0):
                               e.matmul(o, w, x, start=st, stop=False),
                               deps=([Lw, h_done] + bdep(("pb", gp)))
                               if kc == 0 else [])
                MB = p.add("pe", lambda e, o=pb[gp][:, :],
                           w=bt[:, par, :], x=mha[:, ts_]:
                           e.matmul(o, w, x, start=False, stop=True),
                           deps=[Lb, T3])
                MU = None
                for kc in range(16):
                    MU = p.add("pe", lambda e, o=pb[up_][:, :],
                               w=wiu_t[:, par, 1, kc, :],
                               x=h_sb[:, kc, :], st=(kc == 0),
                               sp_=(kc == 15):
                               e.matmul(o, w, x, start=st, stop=sp_),
                               deps=bdep(("pb", up_)) if kc == 0 else [])
                buf_last[("wiu", par)] = [MU]
                buf_last[("bt", par)] = [MB]
                for kk in range(16):
                    buf_last[("h", kk)] = [MU]
                SIL = p.add("act", lambda e, o=g_sb[:, :], src=pb[gp][:, :]:
                            e.activation(o, src, AF.Silu),
                            deps=[MB] + list(buf_last.pop("g_sb", ())))
                GU = p.add("dve", lambda e, o=gu[:, f, :], a=g_sb[:, :],
                           b=pb[up_][:, :]: e.tensor_mul(o, a, b),
                           deps=[SIL, MU])
                buf_last["g_sb"] = [GU]
                bank_last[("pb", gp)] = [SIL]
                bank_last[("pb", up_)] = [GU]
                guf[f] = GU
            S1 = S2 = MM = None
            Lo0 = p.add("sp", lambda e, o=o0t[:, :, :], src=o0sl[:, :, ts_]:
                        e.dma_start(out=o, in_=src),
                        deps=buf_last.pop("o0t", ()), dma=True, slot="o0t")
            for m in range(16):
                par = m % 2
                Lwo = p.add("pool", lambda e, o=wo_t[:, par, :, :],
                            src=wo0_b[m]: e.dma_start(out=o, in_=src),
                            deps=buf_last.pop(("wo", par), ()),
                            dma=True, slot=("wo", par))
                Lx = p.add("sp", lambda e, o=xs[:, par, :],
                           src=xfm[m][:, ts_]: e.dma_start(out=o, in_=src),
                           deps=buf_last.pop(("xs", par), ()),
                           dma=True, slot=("xs", par))
                Lxn = p.add("sp", lambda e, o=xns[:, par, :],
                            src=xnfm[m][:, ts_]: e.dma_start(out=o, in_=src),
                            deps=buf_last.pop(("xns", par), ()),
                            dma=True, slot=("xn", par))
                mp = par
                MM = None
                for kc in range(16):
                    MM = p.add("pe", lambda e, o=pb[mp][:, :],
                               w=wo_t[:, par, kc, :], x=o0t[:, kc, :],
                               st=(kc == 0):
                               e.matmul(o, w, x, start=st, stop=False),
                               deps=([Lwo, Lo0] + bdep(("pb", mp)))
                               if kc == 0 else [])
                for g in range(4):
                    gpar = g % 2
                    Lwd = p.add("pool", lambda e, o=wd_t[:, gpar, :, :],
                                src=wd_b[m][:, g * 16:(g + 1) * 16, :]:
                                e.dma_start(out=o, in_=src),
                                deps=buf_last.pop(("wd", gpar), ()),
                                dma=True, slot=("wd", gpar))
                    for fi in range(16):
                        f = g * 16 + fi
                        MM = p.add("pe", lambda e, o=pb[mp][:, :],
                                   w=wd_t[:, gpar, fi, :], x=gu[:, f, :],
                                   sp_=(f == 63):
                                   e.matmul(o, w, x, start=False, stop=sp_),
                                   deps=[Lwd, guf[f]] if fi == 0
                                   else [guf[f]])
                    buf_last[("wd", gpar)] = [MM]
                Y1 = p.add("dve", lambda e, o=ysc[:, par, :], a=pb[mp][:, :],
                           b=xs[:, par, :]: e.tensor_add(o, a, b),
                           deps=[MM, Lx] +
                           list(buf_last.pop(("ysc", par), ())))
                bank_last[("pb", mp)] = [Y1]
                Y2 = p.add("pool", lambda e, o=ysc[:, par, :],
                           b=xns[:, par, :]: e.tensor_add(o, o, b),
                           deps=[Y1, Lxn])
                buf_last[("xs", par)] = [Y1]
                buf_last[("xns", par)] = [Y2]
                SY = p.add("sp", lambda e, o=yfm[m][:, ts_],
                           src=ysc[:, par, :]: e.dma_start(out=o, in_=src),
                           deps=[Y2], dma=True, slot=("sy", par))
                buf_last[("ysc", par)] = [SY]
                sy_last[par] = SY
                YB = p.add("act", lambda e, o=ybf[:, :], src=ysc[:, par, :]:
                           e.copy(o, src),
                           deps=[Y2] + list(buf_last.pop("ybf", ())))
                S1 = p.add("pe", lambda e, o=st2[0:1, 0:512], w=ocol[:, :],
                           x=ybf[:, :], st=(m == 0), sp_=(m == 15):
                           e.matmul(o, w, x, start=st, stop=sp_),
                           deps=([Loc, YB] + bdep("st")) if m == 0
                           else [YB])
                SQ = p.add("act", lambda e, o=ybf[:, :]:
                           e.activation(o, o, AF.Square), deps=[S1])
                S2 = p.add("pe", lambda e, o=st2[0:1, 512:1024],
                           w=ocol[:, :], x=ybf[:, :],
                           st=(m == 0), sp_=(m == 15):
                           e.matmul(o, w, x, start=st, stop=sp_),
                           deps=[SQ])
                buf_last["ybf"] = [S2]
            buf_last["o0t"] = [MM]
            E1 = p.add("dve", lambda e, o=rows[:, ti, :], src=st2[0:1, :]:
                       e.tensor_copy(o, src), deps=[S2])
            bank_last["st"] = [E1]
            stats_done.append(E1)

        eps1, eps1_st = cns(1e-5)
        mh, mh_st = cns(-0.5)
        R0 = p.add("dve", lambda e, o=rows[:, :, :]:
                   e.tensor_single_scalar(o, o, 1.0 / D, ALU.mult),
                   deps=stats_done)
        msq = ysc[0:1, 0, :]
        mr2 = mods[0:1, 0:1024]
        xstores = []
        vsteps = []
        for ti in range(2):
            Vq = p.add("dve", lambda e, o=msq, a=rows[:, ti, 0:512]:
                       e.tensor_mul(o, a, a),
                       deps=[R0] + ([sy_last[0], sy_last[1]]
                                    if ti == 0 else []))
            Vv = p.add("dve", lambda e, o=rows[:, ti, 512:1024], b=msq:
                       e.tensor_sub(o, o, b))
            Aln = p.add("act", lambda e, o=rows[:, ti, 512:1024],
                        b=eps1[0:1, :]: e.activation(o, o, AF.Ln, bias=b),
                        deps=[Vv, eps1_st])
            Aex = p.add("act", lambda e, o=rows[:, ti, 512:1024],
                        sc=mh[0:1, :]: e.activation(o, o, AF.Exp, scale=sc),
                        deps=[mh_st])
            Vm = p.add("dve", lambda e, o=mr2[:, ti * 512:(ti + 1) * 512],
                       m_=rows[:, ti, 0:512], r=rows[:, ti, 512:1024]:
                       e.tensor_mul(o, m_, r), deps=[Aex])
            vsteps.append(Vm)
        bsteps = []
        for r_i in range(2):
            cps = []
            for ti in range(2):
                hs = slice(ti * 512, (ti + 1) * 512)
                src_row = (rows[:, ti, 512:1024] if r_i == 0
                           else mr2[:, ti * 512:(ti + 1) * 512])
                CP = p.add("act", lambda e, o=rbf[0:1, hs], s_=src_row:
                           e.copy(o, s_),
                           deps=vsteps + list(buf_last.pop("rbf", ())))
                cps.append(CP)
            for half in range(2):
                hs = slice(half * 512, (half + 1) * 512)
                bk = 2 * r_i + half
                BM = p.add("pe", lambda e, o=pb[bk][:, :], w=orow[0:1, :],
                           x=rbf[0:1, hs]:
                           e.matmul(o, w, x, start=True, stop=True),
                           deps=[cps[half], Lor] + bdep(("pb", bk)))
                EB = p.add("dve", lambda e, o=bcst[:, r_i, hs],
                           src=pb[bk][:, :]: e.tensor_copy(o, src),
                           deps=[BM])
                bank_last[("pb", bk)] = [EB]
                bsteps.append(EB)
            buf_last["rbf"] = [BM]
        first_d = True
        for m in range(16):
            for half in range(2):
                par = half
                hs = slice(half * 512, (half + 1) * 512)
                deps = list(buf_last.pop(("xs", par), ()))
                if first_d:
                    deps += [sy_last[0], sy_last[1]]
                Ly = p.add("sp", lambda e, o=xs[:, par, :],
                           src=yfm[m][:, hs]: e.dma_start(out=o, in_=src),
                           deps=deps, dma=True, slot=("xs", par))
                X1 = p.add("dve", lambda e, o=xs[:, par, :],
                           b=bcst[:, 0, hs]: e.tensor_mul(o, o, b),
                           deps=[Ly] + bsteps)
                X2 = p.add("pool", lambda e, o=xns[:, par, :],
                           a=xs[:, par, :], b=bcst[:, 1, hs]:
                           e.tensor_sub(o, a, b),
                           deps=[X1] + list(buf_last.pop(("xns", par), ())))
                buf_last[("xs", par)] = [X1, X2]
                SX = p.add("sp", lambda e, o=xn2fm[m][:, hs],
                           src=xns[:, par, :]: e.dma_start(out=o, in_=src),
                           deps=[X2], dma=True, slot=("sx", par))
                buf_last[("xns", par)] = [SX]
                xstores.append(SX)
            first_d = False
        p.add("sp", None, deps=[xstores[-1], xstores[-2]])
        p.emit(nc, block, cx.es)
    return nc



def build_p4(seq=S):
    nb = seq // 512
    nc = bass.Bass("TRN2", target_bir_lowering=False)
    ap = lambda n, s, d, k="ExternalInput": nc.dram_tensor(n, s, d, kind=k).ap()
    xnb = ap("xnb", [nb, 128, 16, 512], BF16)
    wq1 = ap("wq1", [16, 128, 256], F32)
    wk1 = ap("wk1", [16, 128, 256], F32)
    wv1 = ap("wv1", [16, 128, 256], F32)
    bq1 = ap("bq1", [128, 2], F32)
    bk1 = ap("bk1", [128, 2], F32)
    bv1 = ap("bv1", [128, 2], F32)
    o1 = ap("o1", [2, 128, seq], BF16, "ExternalOutput")

    p = Plan()
    with contextlib.ExitStack() as es:
        cx = Ctx(nc, es)
        fb = [cx.psum(f"fb{i}", [128, 512], F32) for i in range(6)]
        xin = cx.sbuf("xin", [128, 2, 16, 512], BF16)
        w_sb = cx.sbuf("w_sb", [128, 3, 16, 256], BF16)
        b_sb = cx.sbuf("b_sb", [128, 3, 2], F32)
        q_sb = cx.sbuf("q_sb", [128, seq], BF16)
        k_sb = cx.sbuf("k_sb", [128, seq], BF16)
        v_sb = cx.sbuf("v_sb", [128, seq], BF16)
        kv_sb = cx.sbuf("kv_sb", [128, seq], BF16)
        sc1 = cx.sbuf("sc1", [128, seq], BF16)
        block = cx.block()

        bank_last = {}
        buf_last = {}

        def bdep(k):
            return list(bank_last.get(k, ()))

        wsteps = []
        for t_i, w_ap in enumerate((wq1, wk1, wv1)):
            for kc in range(16):
                dep = [buf_last.get("last_pe")] if not wsteps else []
                wsteps.append(p.add("pool", lambda e, o=w_sb[:, t_i, kc, :],
                                    src=w_ap[kc]: e.dma_start(out=o, in_=src),
                                    deps=dep, dma=True, slot="w"))
        for t_i, b_ap in enumerate((bq1, bk1, bv1)):
            wsteps.append(p.add("pool", lambda e, o=b_sb[:, t_i, :], src=b_ap:
                                e.dma_start(out=o, in_=src), dma=True,
                                slot="w"))
        w_ready = wsteps[-1]

        stores = {0: None, 1: None}
        for h in range(2):
            hc = slice(h * 128, (h + 1) * 128)
            proj_last = {}

            def padd(eng, fn, deps=()):
                st = p.add(eng, fn, deps)
                proj_last[eng] = st
                return st

            for n in range(nb):
                par = n % 2
                ncol = slice(n * 512, (n + 1) * 512)
                Lx = p.add("sp", lambda e, o=xin[:, par, :, :], src=xnb[n]:
                           e.dma_start(out=o, in_=src),
                           deps=buf_last.pop(("xin", par), ()),
                           dma=True, slot=("xin", par))
                mm = {}
                for m in range(3):
                    for kc in range(16):
                        mm[m] = p.add("pe", lambda e, o=fb[m][:, :],
                                      w=w_sb[:, m, kc, hc],
                                      x=xin[:, par, kc, :], st=(kc == 0),
                                      sp_=(kc == 15):
                                      e.matmul(o, w, x, start=st, stop=sp_),
                                      deps=[Lx, w_ready] + bdep(("f", m))
                                      if kc == 0 else [])
                for m, dest in ((0, q_sb), (1, k_sb), (2, v_sb)):
                    ev = padd("act", lambda e, o=dest[:, ncol],
                              src=fb[m][:, :], b=b_sb[:, m, h:h + 1]:
                              e.activation(o, src, AF.Identity, bias=b),
                              deps=[mm[m]] +
                              list(buf_last.pop(("qkv", m), ())))
                    bank_last[("f", m)] = [ev]
                buf_last[("xin", par)] = [mm[2]]
            evs_done = [s for s in proj_last.values()]

            phis = []
            for src_sb in (q_sb, k_sb):
                P1_ = p.add("dve", lambda e, o=sc1[:, :], s_=src_sb[:, :]:
                            e.tensor_scalar_min(o, s_, 0.0),
                            deps=evs_done + list(buf_last.pop("sc1", ())))
                P2_ = p.add("act", lambda e, o=sc1[:, :]:
                            e.activation(o, o, AF.Exp), deps=[P1_])
                P3_ = p.add("dve", lambda e, o=src_sb[:, :]:
                            e.tensor_scalar_max(o, o, 0.0), deps=[P1_])
                P4_ = p.add("dve", lambda e, o=src_sb[:, :], b=sc1[:, :]:
                            e.tensor_add(o, o, b), deps=[P2_])
                buf_last["sc1"] = [P4_]
                phis.append(P4_)
            U = p.add("dve", lambda e, o=v_sb[:, :], a=k_sb[:, :]:
                      e.tensor_mul(o, o, a), deps=phis)
            KV = p.add("dve", lambda e, o=kv_sb[:, :], d0=v_sb[:, :]:
                       e.tensor_tensor_scan(o, d0, d0, 0.0, ALU.add,
                                            ALU.bypass),
                       deps=[U] + list(buf_last.pop("kv", ())))
            CX = p.add("dve", lambda e, o=kv_sb[:, :], a=q_sb[:, :]:
                       e.tensor_mul(o, o, a), deps=[KV])
            CS = p.add("pool", lambda e, o=kv_sb[:, :]:
                       e.tensor_scalar_mul(o, o[:, :], SCL), deps=[CX])
            OS = p.add("pool", lambda e, o=o1[h][:, :], src=kv_sb[:, :]:
                       e.dma_start(out=o, in_=src),
                       deps=[CS], dma=True, slot=("st", h))
            stores[h] = OS
            buf_last["kv"] = [OS]
            buf_last[("qkv", 0)] = [CX]
            buf_last[("qkv", 1)] = [U]
            buf_last[("qkv", 2)] = [KV]
            buf_last["last_pe"] = mm[2]
        p.add("pool", None, deps=[stores[0], stores[1]])
        p.emit(nc, block, cx.es)
    return nc


def p4_weights(Wq, bq, Wk, bk, Wv, bv, core):
    h0 = 2 * core
    out = {}
    for name, W, b in (("q1", Wq, bq), ("k1", Wk, bk), ("v1", Wv, bv)):
        sl = np.ascontiguousarray(W[h0 * 128:(h0 + 2) * 128, :].T)
        out["w" + name] = sl.reshape(16, 128, 256)
        out["b" + name] = np.ascontiguousarray(
            b[h0 * 128:(h0 + 2) * 128].reshape(2, 128).T)
    return out


def np_linattn(xn, wq, bq, wk, bk, wv, bv):
    f32 = np.float32
    X = np.asarray(xn, f32).T
    q = X @ np.asarray(wq, f32) + bq[None, :]
    k = X @ np.asarray(wk, f32) + bk[None, :]
    v = X @ np.asarray(wv, f32) + bv[None, :]
    phi_q = np.where(q > 0, q + 1, np.exp(np.minimum(q, 0)))
    phi_k = np.where(k > 0, k + 1, np.exp(np.minimum(k, 0)))
    kv = np.cumsum(phi_k * v, axis=0)
    return (phi_q * kv).T * SCL



def build_p5():
    nc = bass.Bass("TRN2", target_bir_lowering=False)
    ap = lambda n, s, d, k="ExternalInput": nc.dram_tensor(n, s, d, kind=k).ap()
    yfm = ap("yfm", [16, 128, TPC], F32)
    xn2pm = ap("xn2pm", [128, 16, TPC], BF16)
    o1sl = ap("o1sl", [128, 16, TPC], BF16)
    wo1_b = ap("wo1_b", [16, 128, 16, 128], F32)
    bo_pc = ap("bo_pc", [128, 16], F32)
    wr_b = ap("wr_b", [128, 16, 8], F32)
    wiu_b = ap("wiu_b", [64, 128, 2, 16, 128], F32)
    wd_b = ap("wd_b", [16, 128, 64, 128], F32)
    riu_b = ap("riu_b", [8, 128, 2, 16, 128], F32)
    rd_b = ap("rd_b", [16, 128, 8, 128], F32)
    idb = ap("idb", [128, 128], BF16)
    idf = ap("idf", [128, 128], F32)
    onesrow = ap("onesrow", [1, 128], BF16)
    xout = ap("xout", [TPC, D], F32, "ExternalOutput")

    p = Plan()
    with contextlib.ExitStack() as es:
        cx = Ctx(nc, es)
        pb = [cx.psum(f"pb{i}", [128, 512], F32) for i in range(4)]
        lg = cx.psum("lg", [8, 1024], F32)
        bb = cx.psum("bbk", [128, 1024], BF16)
        gu = cx.sbuf("gu", [128, 64, 512], BF16)
        gur = cx.sbuf("gur", [128, 8, 512], BF16)
        xn2t = cx.sbuf("xn2t", [128, 16, 512], BF16)
        o1t = cx.sbuf("o1t", [128, 16, 512], BF16)
        wiu_t = cx.sbuf("wiu_t", [128, 2, 2, 16, 128], BF16)
        wd_t = cx.sbuf("wd_t", [128, 2, 16, 128], BF16)
        wo1_t = cx.sbuf("wo1_t", [128, 2, 16, 128], BF16)
        rd_t = cx.sbuf("rd_t", [128, 2, 8, 128], BF16)
        wr_sb = cx.sbuf("wr_sb", [128, 16, 8], BF16)
        bo_s = cx.sbuf("bo_s", [128, 16], F32)
        ids_b = cx.sbuf("ids_b", [128, 128], BF16)
        ids_f = cx.sbuf("ids_f", [128, 128], F32)
        orow = cx.sbuf("orow", [1, 128], BF16)
        lg_sb = cx.sbuf("lg_sb", [8, 512], BF16)
        lgtm = cx.sbuf("lgtm", [128, 4, 8], F32)
        cw = cx.sbuf("cw", [128, 16], F32)
        cwbf = cx.sbuf("cwbf", [128, 1], BF16)
        crow = cx.sbuf("crow", [1, 512], BF16)
        cwb = cx.sbuf("cwb", [128, 512], BF16)
        g_sb = cx.sbuf("g_sb", [128, 512], BF16)
        ysc = cx.sbuf("ysc", [128, 2, 512], F32)
        ys = cx.sbuf("ys", [128, 2, 512], F32)
        otm = cx.sbuf("otm", [128, 2, 2, 128], F32)
        block = cx.block()

        bank_last = {}
        buf_last = {}

        def bdep(k):
            return list(bank_last.get(k, ()))

        Lwr = p.add("pool", lambda e: e.dma_start(out=wr_sb[:, :, :],
                                                  in_=wr_b),
                    dma=True, slot="wr")
        Lbo = p.add("sp", lambda e: e.dma_start(out=bo_s[:, :], in_=bo_pc),
                    dma=True, slot="bo")
        Lib = p.add("sp", lambda e: e.dma_start(out=ids_b[:, :], in_=idb),
                    dma=True, slot="ib")
        Lif = p.add("sp", lambda e: e.dma_start(out=ids_f[:, :], in_=idf),
                    dma=True, slot="if")
        Lor = p.add("sp", lambda e: e.dma_start(out=orow[:, :], in_=onesrow),
                    dma=True, slot="or")

        xst = []
        for ti in range(2):
            ts_ = slice(ti * 512, (ti + 1) * 512)
            Lx2 = p.add("sp", lambda e, o=xn2t[:, :, :],
                        src=xn2pm[:, :, ts_]: e.dma_start(out=o, in_=src),
                        deps=buf_last.pop("xn2t", ()), dma=True, slot="x2")
            Lo1 = p.add("sp", lambda e, o=o1t[:, :, :], src=o1sl[:, :, ts_]:
                        e.dma_start(out=o, in_=src),
                        deps=buf_last.pop("o1t", ()), dma=True, slot="o1")
            RM = None
            for kc in range(16):
                RM = p.add("pe", lambda e, o=lg[:, 0:512],
                           w=wr_sb[:, kc, :], x=xn2t[:, kc, :],
                           st=(kc == 0), sp_=(kc == 15):
                           e.matmul(o, w, x, start=st, stop=sp_),
                           deps=([Lwr, Lx2] + bdep("lg")) if kc == 0 else [])
            RE = p.add("act", lambda e, o=lg_sb[0:8, :], src=lg[0:8, 0:512]:
                       e.copy(o, src),
                       deps=[RM] + list(buf_last.pop("lg_sb", ())))
            bank_last["lg"] = [RE]
            crow_evs = []
            for q in range(4):
                TQ = p.add("pe", lambda e, o=bb[:, 0:8].bitcast(F32)
                           if False else bb[:, 0:8],
                           src=lg_sb[0:8, q * 128:(q + 1) * 128],
                           idn=ids_b[0:8, 0:8]:
                           e.transpose(o, src, idn),
                           deps=[RE, Lib] + bdep("bb"))
                TE = p.add("dve", lambda e, o=lgtm[:, q, :], src=bb[:, 0:8]:
                           e.tensor_copy(o, src), deps=[TQ])
                bank_last["bb"] = [TE]
                V1 = p.add("dve", lambda e, o=cw[:, 0:1], src=lgtm[:, q, :]:
                           e.reduce_max(o, src, axis=AX.X))
                EQ = p.add("dve", lambda e, o=cw[:, 8:16],
                           src=lgtm[:, q, :], v=cw[:, 0:1]:
                           e.tensor_scalar(o, src, v, 30000.0,
                                           ALU.is_equal, ALU.mult))
                M1 = p.add("dve", lambda e, o=cw[:, 8:16],
                           a=lgtm[:, q, :]: e.tensor_sub(o, a, o))
                M2 = p.add("dve", lambda e, o=cw[:, 2:3], src=cw[:, 8:16]:
                           e.reduce_max(o, src, axis=AX.X, negate=True))
                D1 = p.add("dve", lambda e, o=cw[:, 3:4], a=cw[:, 2:3],
                           b=cw[:, 0:1]: e.tensor_add(o, a, b))
                D2 = p.add("dve", lambda e, o=cw[:, 3:4]:
                           e.tensor_single_scalar(o, o, -1.0, ALU.mult))
                EX = p.add("act", lambda e, o=cw[:, 4:5], src=cw[:, 3:4]:
                           e.activation(o, src, AF.Exp), deps=[D2])
                DN = p.add("dve", lambda e, o=cw[:, 5:6], src=cw[:, 4:5]:
                           e.tensor_single_scalar(o, src, 1.0, ALU.add),
                           deps=[EX])
                RC = p.add("dve", lambda e, o=cw[:, 5:6]:
                           e.reciprocal(o, o))
                CB = p.add("dve", lambda e, o=cw[:, 3:4], src=cw[:, 4:5],
                           r=cw[:, 5:6]:
                           e.tensor_scalar(o, src, r, r, ALU.mult, ALU.add))
                CC = p.add("act", lambda e, o=cwbf[:, :], src=cw[:, 3:4]:
                           e.copy(o, src),
                           deps=[CB] + list(buf_last.pop("cwbf", ())))
                TC = p.add("pe", lambda e, o=bb[0:1, 512:640],
                           src=cwbf[:, :], idn=ids_b[:, :]:
                           e.transpose(o, src, idn),
                           deps=[CC, Lib] + bdep("bb2"))
                CE = p.add("dve", lambda e, o=crow[0:1, q * 128:(q + 1) * 128],
                           src=bb[0:1, 512:640]: e.tensor_copy(o, src),
                           deps=[TC] + list(buf_last.pop("crow", ())))
                bank_last["bb2"] = [CE]
                buf_last["cwbf"] = [TC]
                crow_evs.append(CE)
            BC = p.add("pe", lambda e, o=pb[3][:, :], w=orow[0:1, :],
                       x=crow[0:1, :]: e.matmul(o, w, x, start=True,
                                                stop=True),
                       deps=crow_evs + [Lor] + bdep(("pb", 3)))
            buf_last["crow"] = [BC]
            CWE = p.add("dve", lambda e, o=cwb[:, :], src=pb[3][:, :]:
                        e.tensor_copy(o, src),
                        deps=[BC] + list(buf_last.pop("cwb_w", ())))
            bank_last[("pb", 3)] = [CWE]

            guf = {}
            for f in range(64):
                par = f % 2
                Lw = p.add("pool", lambda e, o=wiu_t[:, par, :, :, :],
                           src=wiu_b[f]: e.dma_start(out=o, in_=src),
                           deps=buf_last.pop(("wiu", par), ()),
                           dma=True, slot=("wiu", par))
                gp, up_ = (0, 1) if f % 2 == 0 else (2, 0)
                gp, up_ = (0, 1) if f % 2 == 0 else (2, 3)
                MG = None
                for kc in range(16):
                    MG = p.add("pe", lambda e, o=pb[gp][:, :],
                               w=wiu_t[:, par, 0, kc, :],
                               x=xn2t[:, kc, :], st=(kc == 0),
                               sp_=(kc == 15):
                               e.matmul(o, w, x, start=st, stop=sp_),
                               deps=([Lw, Lx2] + bdep(("pb", gp)))
                               if kc == 0 else [])
                MU = None
                for kc in range(16):
                    MU = p.add("pe", lambda e, o=pb[up_][:, :],
                               w=wiu_t[:, par, 1, kc, :],
                               x=xn2t[:, kc, :], st=(kc == 0),
                               sp_=(kc == 15):
                               e.matmul(o, w, x, start=st, stop=sp_),
                               deps=bdep(("pb", up_)) if kc == 0 else [])
                buf_last[("wiu", par)] = [MU]
                SIL = p.add("act", lambda e, o=g_sb[:, :], src=pb[gp][:, :]:
                            e.activation(o, src, AF.Silu),
                            deps=[MG] + list(buf_last.pop("g_sb", ())))
                GU = p.add("dve", lambda e, o=gu[:, f, :], a=g_sb[:, :],
                           b=pb[up_][:, :]: e.tensor_mul(o, a, b),
                           deps=[SIL, MU])
                buf_last["g_sb"] = [GU]
                bank_last[("pb", gp)] = [SIL]
                bank_last[("pb", up_)] = [GU]
                guf[f] = GU
            gurf = {}
            for f in range(8):
                par = f % 2
                Lw = p.add("pool", lambda e, o=wiu_t[:, par, :, :, :],
                           src=riu_b[f]: e.dma_start(out=o, in_=src),
                           deps=buf_last.pop(("wiu", par), ()),
                           dma=True, slot=("wiu", par))
                gp, up_ = (0, 1) if f % 2 == 0 else (2, 3)
                MG = None
                for kc in range(16):
                    MG = p.add("pe", lambda e, o=pb[gp][:, :],
                               w=wiu_t[:, par, 0, kc, :],
                               x=xn2t[:, kc, :], st=(kc == 0),
                               sp_=(kc == 15):
                               e.matmul(o, w, x, start=st, stop=sp_),
                               deps=([Lw] + bdep(("pb", gp)))
                               if kc == 0 else [])
                MU = None
                for kc in range(16):
                    MU = p.add("pe", lambda e, o=pb[up_][:, :],
                               w=wiu_t[:, par, 1, kc, :],
                               x=xn2t[:, kc, :], st=(kc == 0),
                               sp_=(kc == 15):
                               e.matmul(o, w, x, start=st, stop=sp_),
                               deps=bdep(("pb", up_)) if kc == 0 else [])
                buf_last[("wiu", par)] = [MU]
                SIL = p.add("act", lambda e, o=g_sb[:, :], src=pb[gp][:, :]:
                            e.activation(o, src, AF.Silu),
                            deps=[MG] + list(buf_last.pop("g_sb", ())))
                GU = p.add("dve", lambda e, o=gur[:, f, :], a=g_sb[:, :],
                           b=pb[up_][:, :]: e.tensor_mul(o, a, b),
                           deps=[SIL, MU])
                GW = p.add("pool", lambda e, o=gur[:, f, :], b=cwb[:, :]:
                           e.tensor_mul(o, o, b), deps=[GU, CWE])
                buf_last["g_sb"] = [GU]
                bank_last[("pb", gp)] = [SIL]
                bank_last[("pb", up_)] = [GU]
                gurf[f] = GW
            buf_last["cwb_w"] = [GW]
            SO = None
            for m in range(16):
                par = m % 2
                Lwo = p.add("pool", lambda e, o=wo1_t[:, par, :, :],
                            src=wo1_b[m]: e.dma_start(out=o, in_=src),
                            deps=buf_last.pop(("wo", par), ()),
                            dma=True, slot=("wo", par))
                Lrd = p.add("pool", lambda e, o=rd_t[:, par, :, :],
                            src=rd_b[m]: e.dma_start(out=o, in_=src),
                            deps=buf_last.pop(("rd", par), ()),
                            dma=True, slot=("rd", par))
                Ly = p.add("sp", lambda e, o=ys[:, par, :],
                           src=yfm[m][:, ts_]: e.dma_start(out=o, in_=src),
                           deps=buf_last.pop(("ys", par), ()),
                           dma=True, slot=("ys", par))
                mp = par
                MM = None
                for kc in range(16):
                    MM = p.add("pe", lambda e, o=pb[mp][:, :],
                               w=wo1_t[:, par, kc, :], x=o1t[:, kc, :],
                               st=(kc == 0):
                               e.matmul(o, w, x, start=st, stop=False),
                               deps=([Lwo, Lo1] + bdep(("pb", mp)))
                               if kc == 0 else [])
                for g in range(4):
                    gpar = g % 2
                    Lwd = p.add("pool", lambda e, o=wd_t[:, gpar, :, :],
                                src=wd_b[m][:, g * 16:(g + 1) * 16, :]:
                                e.dma_start(out=o, in_=src),
                                deps=buf_last.pop(("wd", gpar), ()),
                                dma=True, slot=("wd", gpar))
                    for fi in range(16):
                        f = g * 16 + fi
                        MM = p.add("pe", lambda e, o=pb[mp][:, :],
                                   w=wd_t[:, gpar, fi, :], x=gu[:, f, :]:
                                   e.matmul(o, w, x, start=False,
                                            stop=False),
                                   deps=[Lwd, guf[f]] if fi == 0
                                   else [guf[f]])
                    buf_last[("wd", gpar)] = [MM]
                for f in range(8):
                    MM = p.add("pe", lambda e, o=pb[mp][:, :],
                               w=rd_t[:, par, f, :], x=gur[:, f, :],
                               sp_=(f == 7):
                               e.matmul(o, w, x, start=False, stop=sp_),
                               deps=[Lrd, gurf[f]] if f == 0 else [gurf[f]])
                buf_last[("rd", par)] = [MM]
                Y1 = p.add("dve", lambda e, o=ysc[:, par, :], a=pb[mp][:, :],
                           b=bo_s[:, m:m + 1], c=ys[:, par, :]:
                           e.scalar_tensor_tensor(o, a, b, c, ALU.add,
                                                  ALU.add),
                           deps=[MM, Ly, Lbo] +
                           list(buf_last.pop(("ysc", par), ())))
                bank_last[("pb", mp)] = [Y1]
                buf_last[("ys", par)] = [Y1]
                for q in range(4):
                    tb = 2 + (q % 2)
                    TT_ = p.add("pe", lambda e, o=pb[tb][:, 0:128],
                                src=ysc[:, par, q * 128:(q + 1) * 128],
                                idn=ids_f[:, :]:
                                e.transpose(o, src, idn),
                                deps=[Y1, Lif] + bdep(("pb", tb)))
                    TE_ = p.add("dve", lambda e, o=otm[:, q % 2, par, :],
                                src=pb[tb][:, 0:128]: e.tensor_copy(o, src),
                                deps=[TT_] +
                                list(buf_last.pop(("otm", q % 2, par), ())))
                    bank_last[("pb", tb)] = [TE_]
                    SO = p.add("sp", lambda e,
                               o=xout[ti * 512 + q * 128:
                                      ti * 512 + (q + 1) * 128,
                                      m * 128:(m + 1) * 128],
                               src=otm[:, q % 2, par, :]:
                               e.dma_start(out=o, in_=src), deps=[TE_],
                               dma=True, slot=("so", q % 2, par))
                    buf_last[("otm", q % 2, par)] = [SO]
                buf_last[("ysc", par)] = [TT_]
            buf_last["xn2t"] = [MM]
            buf_last["o1t"] = [MM]
            buf_last["lg_sb"] = [TQ]
            xst.append(SO)
        fin = [buf_last[("otm", a, b)][0] for a in range(2) for b in range(2)]
        p.add("sp", None, deps=fin)
        p.emit(nc, block, cx.es)
    return nc


def p5_host_arrays(Wo1, bo, router_w, sh_gate, sh_up, sh_down,
                   r_gate, r_up, r_down):
    out = {}
    Wo1T = np.ascontiguousarray(Wo1.T)
    out["wo1_b"] = np.ascontiguousarray(
        Wo1T.reshape(16, 128, 16, 128).transpose(2, 1, 0, 3))
    out["bo_pc"] = np.ascontiguousarray(bo.reshape(16, 128).T)
    out["wr_b"] = np.ascontiguousarray(
        np.ascontiguousarray(router_w.T).reshape(
            16, 128, 8).transpose(1, 0, 2))
    sg = np.ascontiguousarray(sh_gate.T).reshape(16, 128, 64, 128)
    su = np.ascontiguousarray(sh_up.T).reshape(16, 128, 64, 128)
    out["wiu_b"] = np.ascontiguousarray(
        np.stack([sg.transpose(2, 1, 0, 3), su.transpose(2, 1, 0, 3)],
                 axis=1).transpose(0, 2, 1, 3, 4))
    out["wd_b"] = np.ascontiguousarray(
        np.ascontiguousarray(sh_down.T).reshape(
            64, 128, 16, 128).transpose(2, 1, 0, 3))
    rg = np.ascontiguousarray(r_gate.T).reshape(16, 128, 8, 128)
    ru = np.ascontiguousarray(r_up.T).reshape(16, 128, 8, 128)
    out["riu_b"] = np.ascontiguousarray(
        np.stack([rg.transpose(2, 1, 0, 3), ru.transpose(2, 1, 0, 3)],
                 axis=1).transpose(0, 2, 1, 3, 4))
    out["rd_b"] = np.ascontiguousarray(
        np.ascontiguousarray(r_down.T).reshape(
            8, 128, 16, 128).transpose(2, 1, 0, 3))
    out["idb"] = _ident_bf16()
    out["idf"] = _ident_f32()
    out["onesrow"] = np.ones((1, 128), dtype=BF)
    return out


def np_mix1(y_fm, xn2_fm, o1_fm, Wo1, bo, router_w, sh_gate, sh_up, sh_down,
            r_gate, r_up, r_down):
    f32 = np.float32
    Y = np.asarray(y_fm, f32)
    XN = np.asarray(xn2_fm, f32)
    O1 = np.asarray(o1_fm, f32)
    attn1 = np.asarray(Wo1, f32) @ O1 + np.asarray(bo, f32)[:, None]
    def mlp(gw, uw, dw):
        g = np.asarray(gw, f32) @ XN
        u = np.asarray(uw, f32) @ XN
        return np.asarray(dw, f32) @ (g / (1 + np.exp(-g)) * u)
    shared = mlp(sh_gate, sh_up, sh_down)
    lo = np.asarray(router_w, f32) @ XN
    srt = np.sort(lo, axis=0)
    v1, v2 = srt[-1], srt[-2]
    e = np.exp(v2 - v1)
    comb = 1.0 / (1.0 + e) + e / (1.0 + e)
    routed = mlp(r_gate, r_up, r_down) * comb[None, :]
    out = Y + attn1 + shared + routed
    return np.ascontiguousarray(out.T)


def p3_host_arrays(Wo0, Wo_f, A, Bm, W_in, W_up, W_down, ln2_w):
    out = {}
    out["wo0_b"] = np.ascontiguousarray(
        Wo0.reshape(16, 128, 16, 128).transpose(2, 1, 0, 3))
    WofT = np.ascontiguousarray(Wo_f.T)
    out["wof_b"] = np.ascontiguousarray(
        WofT.reshape(16, 128, 16, 128).transpose(2, 1, 0, 3))
    WiT = np.ascontiguousarray(W_in.T)
    WuT = np.ascontiguousarray(W_up.T)
    wi = WiT.reshape(16, 128, 64, 128).transpose(2, 1, 0, 3)
    wu = WuT.reshape(16, 128, 64, 128).transpose(2, 1, 0, 3)
    out["wiu_b"] = np.ascontiguousarray(
        np.stack([wi, wu], axis=1).transpose(0, 2, 1, 3, 4))
    WdT = np.ascontiguousarray(W_down.T)
    out["wd_b"] = np.ascontiguousarray(
        WdT.reshape(64, 128, 16, 128).transpose(2, 1, 0, 3))
    out["a_b"] = np.ascontiguousarray(A.reshape(16, 128, 16).transpose(1, 0, 2))
    out["b_b"] = np.ascontiguousarray(Bm)
    out["w2pc"] = np.ascontiguousarray(ln2_w.reshape(16, 128).T)
    out["onescol"] = np.ones((128, 1), dtype=BF)
    out["onesrow"] = np.ones((1, 128), dtype=BF)
    return out


def np_mix0(x_fm, xn_fm, o0_fm, of_fm, rstd2, Wo0, Wo_f, A, Bm, W_in, W_up,
            W_down, ln2_w):
    f32 = np.float32
    X = np.asarray(x_fm, f32)
    XN = np.asarray(xn_fm, f32)
    O0 = np.asarray(o0_fm, f32)
    OF = np.asarray(of_fm, f32)
    attn0 = (O0.T @ np.asarray(Wo0, f32)).T
    attnf = (OF.T @ np.asarray(Wo_f, f32).T).T
    h = XN * np.asarray(ln2_w, f32)[:, None] * np.asarray(rstd2, f32)[None, :]
    attnfA = np.asarray(A, f32).T @ attnf
    mod = 0.1 * np.tanh(attnfA)
    hA = np.asarray(A, f32).T @ h
    mha = mod * hA
    gate_pre = np.asarray(W_in, f32) @ h + np.asarray(Bm, f32).T @ mha
    gate = gate_pre / (1.0 + np.exp(-gate_pre))
    up = np.asarray(W_up, f32) @ h
    mlp = np.asarray(W_down, f32) @ (gate * up)
    y = X + XN + attn0 + mlp
    m2 = y.mean(axis=0, keepdims=True)
    var = y.var(axis=0, keepdims=True)
    xn2 = (y - m2) / np.sqrt(var + 1e-5)
    return y, xn2


def run_p3(per_core_inputs, weights):
    nc = build_p3()
    warr = p3_host_arrays(**weights)
    in_maps = []
    for c in range(NCORES):
        im = dict(per_core_inputs[c])
        im.update(warr)
        in_maps.append(im)
    res = run_bass_kernel_spmd(nc, in_maps, core_ids=list(range(NCORES)))
    return res.results


ROPE_PERM = np.concatenate([np.arange(0, 128, 2), np.arange(1, 128, 2)])


def rope_tables(seq):
    i = np.arange(64, dtype=np.float32)
    inv = (1.0 / (10000.0 ** (2.0 * i / 64.0 / 2.0))).astype(np.float32)
    pos = np.arange(seq, dtype=np.float32)
    th = np.outer(pos, inv).astype(np.float32)
    return (np.ascontiguousarray(np.cos(th).T),
            np.ascontiguousarray(np.sin(th).T))


def rope_tables_dev(seq):
    cos, sin = rope_tables(seq)
    c1 = np.concatenate([cos, cos], axis=0)
    c2 = np.concatenate([-sin, sin], axis=0)
    return (np.ascontiguousarray(c1.astype(BF)),
            np.ascontiguousarray(c2.astype(BF)))


def rope_permm():
    m = np.zeros((128, 128), dtype=BF)
    m[(np.arange(128) + 64) % 128, np.arange(128)] = 1
    return m


def causal_masks_t():
    m = np.zeros((4, 128, 512), dtype=np.float32)
    for u in range(4):
        for kq in range(128):
            m[u, kq, 128 * u + kq:] = 1.0
    return np.ascontiguousarray(m.transpose(1, 0, 2)).astype(BF)


def causal_masks():
    m = np.full((4, 128, 512), NEG, np.float32)
    for r in range(4):
        for i_ in range(128):
            m[r, i_, :r * 128 + i_ + 1] = 0.0
    return np.ascontiguousarray(m.transpose(1, 0, 2))


def blk_cols(a, seq):
    nb = seq // 512
    return np.ascontiguousarray(
        a.reshape(16, 128, nb, 512).transpose(2, 1, 0, 3))


def p2_weights(Wqkv, Wq_f, Wk_f, Wv_f, core):
    h0, h1 = 2 * core, 2 * core + 1
    out = {}
    for name, base, perm in (("wq0", 0, True), ("wk0", D, True),
                             ("wv0", 2 * D, False)):
        cols = []
        for h in (h0, h1):
            idx = np.arange(h * 128, (h + 1) * 128)
            if perm:
                idx = idx[ROPE_PERM]
            cols.append(Wqkv[:, base + idx])
        out[name] = np.ascontiguousarray(
            np.concatenate(cols, axis=1)).reshape(16, 128, 256)
    for name, W in (("wqf", Wq_f), ("wkf", Wk_f), ("wvf", Wv_f)):
        sl = np.ascontiguousarray(W[h0 * 128:(h1 + 1) * 128, :].T)
        out[name] = sl.reshape(16, 128, 256)
    return out


def run_p2(xn_fm, x1_fm, Wqkv, Wq_f, Wk_f, Wv_f, seq=S):
    nc = build_p2(seq=seq)
    cos, sin = rope_tables_dev(seq)
    masks = causal_masks()
    xnb = blk_cols(_bf16(xn_fm).astype(BF), seq)
    x1b = blk_cols(np.asarray(x1_fm).astype(BF), seq)
    in_maps = []
    for c in range(NCORES):
        im = {"xnb": xnb, "x1b": x1b, "cosr": cos, "sinr": sin,
              "maskr": masks, "masksbr": causal_masks_t(),
              "idb": _ident_bf16(), "permm": rope_permm(),
              "onesr": np.ones((1, 128), dtype=BF),
              "onescl": np.ones((128, 1), dtype=BF)}
        im.update(p2_weights(Wqkv, Wq_f, Wk_f, Wv_f, c))
        in_maps.append(im)
    res = run_bass_kernel_spmd(nc, in_maps, core_ids=list(range(NCORES)))
    return res.results


def np_causal_attn(xn, wq, wk, wv, rope, seq):
    X = np.asarray(xn, np.float32).T
    q = X @ np.asarray(wq, np.float32)
    k = X @ np.asarray(wk, np.float32)
    v = X @ np.asarray(wv, np.float32)
    if rope:
        cos, sin = rope_tables(seq)
        c, s_ = cos.T, sin.T
        for t in (q, k):
            x0 = t[:, :64].copy()
            x1 = t[:, 64:].copy()
            t[:, :64] = x0 * c - x1 * s_
            t[:, 64:] = x0 * s_ + x1 * c
    Sm = (q @ k.T) * SCL
    mask = np.triu(np.ones((seq, seq), bool), 1)
    Sm[mask] = -np.inf
    Sm -= Sm.max(axis=1, keepdims=True)
    P = np.exp(Sm)
    P /= P.sum(axis=1, keepdims=True)
    return (P @ v).T


def run_p1(x_full, ln1_w):
    nc = build_p1()
    w1b = np.ascontiguousarray(np.broadcast_to(
        np.asarray(ln1_w, np.float32), (128, D)))
    in_maps = []
    for c in range(NCORES):
        in_maps.append({
            "x_t": np.ascontiguousarray(x_full[c * TPC:(c + 1) * TPC, :]),
            "w1b": w1b,
            "idf": _ident_f32(),
            "idb": _ident_bf16(),
        })
    res = run_bass_kernel_spmd(nc, in_maps, core_ids=list(range(NCORES)))
    return res.results



def kernel(**inputs):
    f = {k: np.asarray(v) for k, v in inputs.items()}
    x = np.ascontiguousarray(f["L0_attn_x"].reshape(S, D))

    r1 = run_p1(x, f["L0_ffn_ln1_w"])

    xn_all = np.concatenate(
        [np.asarray(r1[c]["xnfm"]).reshape(D, TPC) for c in range(NCORES)],
        axis=1)
    x1_all = np.concatenate(
        [np.asarray(r1[c]["x1fm"]).reshape(D, TPC) for c in range(NCORES)],
        axis=1)

    r2 = run_p2(xn_all, x1_all, f["L0_attn_Wqkv"], f["L0_ffn_Wq"],
                f["L0_ffn_Wk"], f["L0_ffn_Wv"])
    o0_all = np.concatenate(
        [np.asarray(r2[c]["o0"]) for c in range(NCORES)], axis=0)
    of_all = np.concatenate(
        [np.asarray(r2[c]["of"]) for c in range(NCORES)], axis=0)

    p3_in = []
    for c in range(NCORES):
        sl = slice(c * TPC, (c + 1) * TPC)
        rstd2 = np.asarray(r1[c]["rstd2c"]).T.reshape(-1)
        p3_in.append({
            "xfm": np.asarray(r1[c]["xfm"]),
            "xnfm": np.asarray(r1[c]["xnfm"]),
            "o0sl": np.ascontiguousarray(o0_all[:, :, sl].transpose(1, 0, 2)),
            "ofsl": np.ascontiguousarray(of_all[:, :, sl].transpose(1, 0, 2)),
            "rstd2b": np.ascontiguousarray(
                np.broadcast_to(rstd2.astype(np.float32), (128, TPC))),
        })
    r3 = run_p3(p3_in, dict(
        Wo0=f["L0_attn_Wo"], Wo_f=f["L0_ffn_Wo"], A=f["L0_ffn_A"],
        Bm=f["L0_ffn_B"], W_in=f["L0_ffn_W_in_base"],
        W_up=f["L0_ffn_W_up_base"], W_down=f["L0_ffn_W_down"],
        ln2_w=f["L0_ffn_ln2_w"]))

    xn2_all = np.concatenate(
        [np.asarray(r3[c]["xn2fm"]).reshape(D, TPC) for c in range(NCORES)],
        axis=1)

    nc4 = build_p4()
    xnb4 = blk_cols(xn2_all, S)
    in4 = []
    for c in range(NCORES):
        im = {"xnb": xnb4}
        im.update(p4_weights(f["L1_attn_Wq"], f["L1_attn_bq"],
                             f["L1_attn_Wk"], f["L1_attn_bk"],
                             f["L1_attn_Wv"], f["L1_attn_bv"], c))
        in4.append(im)
    r4 = run_bass_kernel_spmd(nc4, in4, core_ids=list(range(NCORES))).results
    o1_all = np.concatenate(
        [np.asarray(r4[c]["o1"]) for c in range(NCORES)], axis=0)

    nc5 = build_p5()
    w5 = p5_host_arrays(
        Wo1=f["L1_attn_Wo"], bo=f["L1_attn_bo"],
        router_w=f["L1_ffn_router_weight"], sh_gate=f["L1_ffn_sh_gate"],
        sh_up=f["L1_ffn_sh_up"], sh_down=f["L1_ffn_sh_down"],
        r_gate=f["L1_ffn_r_gate"], r_up=f["L1_ffn_r_up"],
        r_down=f["L1_ffn_r_down"])
    in5 = []
    for c in range(NCORES):
        sl = slice(c * TPC, (c + 1) * TPC)
        im = {
            "yfm": np.asarray(r3[c]["yfm"]),
            "xn2pm": np.ascontiguousarray(
                np.asarray(r3[c]["xn2fm"]).transpose(1, 0, 2)),
            "o1sl": np.ascontiguousarray(o1_all[:, :, sl].transpose(1, 0, 2)),
        }
        im.update(w5)
        in5.append(im)
    r5 = run_bass_kernel_spmd(nc5, in5, core_ids=list(range(NCORES))).results

    out = np.concatenate(
        [np.asarray(r5[c]["xout"], np.float32) for c in range(NCORES)],
        axis=0)
    return out.reshape(1, S, D)
